# version 24
# baseline (speedup 1.0000x reference)
import numpy as np
import ml_dtypes
from contextlib import ExitStack
import concourse.bass as bass
import concourse.mybir as mybir
from concourse.bass_utils import run_bass_kernel_spmd

F32 = mybir.dt.float32
BF16 = mybir.dt.bfloat16
AF = mybir.ActivationFunctionType
OP = mybir.AluOpType
AXX = mybir.AxisListType.X

D = 1024
NEXP = 9
ATTN_C = 4.0

QUEUES = ("pe", "act", "dve", "pool", "sp")
SEMS = ("pe", "act", "dve", "pool", "swd", "hwd")


class Em:

    def __init__(self, nc):
        self.nc = nc
        self.semh = {}
        self.cnt = {}
        self.q = {qn: [] for qn in QUEUES}
        self.hwm = {qn: {} for qn in QUEUES}
        self.res = {}
        self.nops = 0
        self.stack = None

    def sem(self, name):
        if name not in self.semh:
            self.semh[name] = self.stack.enter_context(self.nc.semaphore(name))
            self.cnt[name] = 0
        return name

    def _wait(self, qn, sem, val):
        if val <= 0 or self.hwm[qn].get(sem, 0) >= val:
            return
        self.hwm[qn][sem] = val
        self.q[qn].append(("w", sem, val))

    def _need(self, qn, producer):
        sem, val = producer
        if qn == "pe" and sem == "pe":
            return
        self._wait(qn, sem, val)

    def op(self, qn, fn, reads=(), writes=(), dma=False):
        if dma:
            assert isinstance(dma, str)
            sem = self.sem("D." + dma)
        else:
            sem = self.sem(qn)
        for k in reads:
            r = self.res.get(k)
            if r and r["w"] is not None:
                self._need(qn, r["w"])
        for k in writes:
            r = self.res.get(k)
            if r:
                if r["w"] is not None:
                    self._need(qn, r["w"])
                for rs, rv in r["r"].items():
                    self._need(qn, (rs, rv))
        inc = 16 if dma else 1
        self.cnt[sem] += inc
        tok = (sem, self.cnt[sem])
        self.q[qn].append(("o", fn, sem, inc))
        self.nops += 1
        for k in reads:
            r = self.res.setdefault(k, {"w": None, "r": {}})
            r["r"][sem] = max(r["r"].get(sem, 0), tok[1])
        for k in writes:
            self.res[k] = {"w": tok, "r": {}}
        return tok

    def barrier(self):
        snap = dict(self.cnt)
        for qn in QUEUES:
            for sname, v in snap.items():
                self._wait(qn, sname, v)

    def replay(self, block):
        mapping = {"pe": block.tensor, "act": block.scalar, "dve": block.vector,
                   "pool": block.gpsimd, "sp": block.sync}

        def mk(qn):
            def body(eng):
                for e in self.q[qn]:
                    if e[0] == "w":
                        eng.wait_ge(self.semh[e[1]], e[2])
                    else:
                        _, fn, sem, inc = e
                        fn(eng).then_inc(self.semh[sem], inc)
            return body
        for qn in QUEUES:
            mapping[qn](mk(qn))


def MM(o, l, r, st, sp):
    return lambda e: e.matmul(o, l, r, start=st, stop=sp)


def DMA(d, srcap):
    return lambda e: e.dma_start(out=d, in_=srcap)


def ACOPY(d, srcap):
    return lambda e: e.copy(d, srcap)


def VCOPY(d, srcap):
    return lambda e: e.tensor_copy(d, srcap)


def EXPA(d, srcap, biasap, scaleap):
    return lambda e: e.activation(d, srcap, AF.Exp, bias=biasap, scale=scaleap)


def STT(o, i0, sc, i1, op0, op1, acc=None):
    return lambda e: e.scalar_tensor_tensor(out=o, in0=i0, scalar=sc, in1=i1,
                                            op0=op0, op1=op1, accum_out=acc)


def TS(o, i0, s1, s2, op0, op1=None):
    if op1 is None:
        return lambda e: e.tensor_scalar(out=o, in0=i0, scalar1=s1, scalar2=None, op0=op0)
    return lambda e: e.tensor_scalar(out=o, in0=i0, scalar1=s1, scalar2=s2, op0=op0, op1=op1)


def TT(o, a, b, opx):
    return lambda e: e.tensor_tensor(o, a, b, opx)


def RED(o, i, opx):
    if opx == "max":
        return lambda e: e.reduce_max(o, i, axis=AXX)
    return lambda e: e.reduce_sum(o, i, axis=AXX)


def build_program(S=4096, BC=2, FT=1024, DEBUG=False):
    KC = S // 128
    QT = min(1024, S)
    NQT = S // QT
    NSC = S // 512
    FCH = S // FT
    TSUBS = FT // 128

    nc = bass.Bass("TRN2", target_bir_lowering=False)
    xtm = nc.dram_tensor("x_tm", [BC, S, D], F32, kind="ExternalInput").ap()
    xfmd = nc.dram_tensor("x_fm", [BC, D, S], F32, kind="ExternalInput").ap()
    wq = nc.dram_tensor("wq", [128, 8, D], F32, kind="ExternalInput").ap()
    wk = nc.dram_tensor("wk", [128, 8, D], F32, kind="ExternalInput").ap()
    wv = nc.dram_tensor("wv", [128, 8, D], F32, kind="ExternalInput").ap()
    wo = nc.dram_tensor("wo", [128, 8, D], F32, kind="ExternalInput").ap()
    upw = nc.dram_tensor("upw", [NEXP, 128, 8, D], F32, kind="ExternalInput").ap()
    dnw = nc.dram_tensor("dnw", [NEXP, 128, 8, D], F32, kind="ExternalInput").ap()
    gfm = nc.dram_tensor("gatefmt", [128, 64], F32, kind="ExternalInput").ap()
    ubf = nc.dram_tensor("upbfm", [128, 72], F32, kind="ExternalInput").ap()
    db9d = nc.dram_tensor("downb9", [NEXP, D], F32, kind="ExternalInput").ap()
    identd = nc.dram_tensor("ident", [128, 128], BF16, kind="ExternalInput").ap()
    outd = nc.dram_tensor("out", [BC, S, D], F32, kind="ExternalOutput").ap()
    dbg = {}
    if DEBUG:
        dbg["afm"] = nc.dram_tensor("dbg_afm", [128, 8 * S], BF16, kind="ExternalOutput").ap()
        dbg["x1"] = nc.dram_tensor("dbg_x1", [128, (S // FT) * FT // 128 * D], F32, kind="ExternalOutput").ap()
        dbg["scores"] = nc.dram_tensor("dbg_scores", [128, 8 * (FT // 128)], F32, kind="ExternalOutput").ap()
        dbg["comb"] = nc.dram_tensor("dbg_comb", [128, 8 * (FT // 128)], F32, kind="ExternalOutput").ap()
        dbg["y"] = nc.dram_tensor("dbg_y", [128, (FT // 128) * D], F32, kind="ExternalOutput").ap()
        dbg["K"] = nc.dram_tensor("dbg_K", [128, 4 * S], BF16, kind="ExternalOutput").ap()
        dbg["V"] = nc.dram_tensor("dbg_V", [128, (S // 128) * 528], BF16, kind="ExternalOutput").ap()
        dbg["qp"] = nc.dram_tensor("dbg_qp", [128, 8 * 512], BF16, kind="ExternalOutput").ap()

    em = Em(nc)

    with ExitStack() as top:
        E = top.enter_context
        em.stack = top
        identb = E(nc.sbuf_tensor([128, 128], BF16))
        gfmt = E(nc.sbuf_tensor([128, 64], BF16))
        ubfm = E(nc.sbuf_tensor([128, 72], F32))
        db9 = E(nc.sbuf_tensor([NEXP, D], BF16))
        csc = E(nc.sbuf_tensor([128, 1], F32))
        cbi = E(nc.sbuf_tensor([128, 1], F32))
        ceps = E(nc.sbuf_tensor([128, 1], F32))
        ones64 = E(nc.sbuf_tensor([1, 128], BF16))
        ones11 = E(nc.sbuf_tensor([128, 1], F32))
        obuf = E(nc.sbuf_tensor([65, QT], BF16))
        bcs = E(nc.sbuf_tensor([64, QT], BF16))
        rlb = E(nc.sbuf_tensor([1, QT], BF16))
        TSUBS0 = FT // 128
        psA = E(nc.psum_tensor([128, 1024], F32))
        psB = E(nc.psum_tensor([128, 1024], F32))
        psC = E(nc.psum_tensor([128, 1024], F32))
        psD = E(nc.psum_tensor([128, 1024], F32))
        scores = E(nc.sbuf_tensor([128, 8 * TSUBS0], F32))
        comb = E(nc.sbuf_tensor([128, 8 * TSUBS0], F32))
        combaug = E(nc.sbuf_tensor([128, 9 * TSUBS0], BF16))
        combT = E(nc.sbuf_tensor([NEXP, 128 * TSUBS0], BF16))
        geq = E(nc.sbuf_tensor([128, 8], F32))
        gmask = E(nc.sbuf_tensor([128, 8], F32))
        zsum = E(nc.sbuf_tensor([128, TSUBS0], F32))
        zsq = E(nc.sbuf_tensor([128, TSUBS0], F32))
        gmean = E(nc.sbuf_tensor([128, TSUBS0], F32))
        grstd = E(nc.sbuf_tensor([128, TSUBS0], F32))
        gtmp = E(nc.sbuf_tensor([128, TSUBS0], F32))
        gm1 = E(nc.sbuf_tensor([128, TSUBS0], F32))
        gm2 = E(nc.sbuf_tensor([128, TSUBS0], F32))
        gsc1 = E(nc.sbuf_tensor([128, TSUBS0], F32))
        em.op("sp", DMA(identb[:], identd), writes=["ident"], dma="ident")
        em.op("pool", DMA(gfmt[:], gfm), writes=["gfmt"], dma="gfmt")
        em.op("sp", DMA(ubfm[:], ubf), writes=["ubfm"], dma="ubfm")
        em.op("pool", DMA(db9[:], db9d), writes=["db9"], dma="db9")
        em.op("dve", (lambda e: e.memset(csc[:], 0.125)), writes=["csc"])
        em.op("dve", (lambda e: e.memset(cbi[:], -ATTN_C)), writes=["cbi"])
        em.op("dve", (lambda e: e.memset(ceps[:], 1e-5)), writes=["ceps"])
        em.op("dve", (lambda e: e.memset(ones64[:], 1.0)), writes=["ones64"])
        em.op("dve", (lambda e: e.memset(ones11[:], 1.0)), writes=["ones11"])

        for s in range(BC):
            with ExitStack() as seqscope:
                afm = seqscope.enter_context(nc.sbuf_tensor([128, 8 * S], BF16))

                with ExitStack() as ph:
                    EA = ph.enter_context
                    Kh = EA(nc.sbuf_tensor([128, 4 * S], BF16))
                    Va = EA(nc.sbuf_tensor([128, KC * 528], BF16))
                    wkh = EA(nc.sbuf_tensor([128, 8 * 512], BF16))
                    wvh = EA(nc.sbuf_tensor([128, 8 * 512], BF16))
                    wqh = EA(nc.sbuf_tensor([128, 8 * 512], BF16))
                    xfm = EA(nc.sbuf_tensor([128, 8192], BF16))
                    qp = EA(nc.sbuf_tensor([128, 8 * QT], BF16))
                    pbuf = EA(nc.sbuf_tensor([128, 3 * QT], BF16))



                    em.op("dve", (lambda q=qp: (lambda e: e.memset(q[:], 0.0)))(),
                          writes=[f"qp.{i}" for i in range(8)])
                    em.op("dve", (lambda q=Va: (lambda e: e.memset(q[:], 0.0)))(), writes=["V"])
                    va_ones = Va.ap().rearrange("p (k d) -> p k d", d=66)[:, :, 64:65]
                    em.op("dve", (lambda a=va_ones: (lambda e: e.memset(a, 1.0)))(), writes=["V"])

                    for half in range(2):
                        h0 = half * 512
                        for (wdst, wsrc, wkey) in ((wkh, wk, "wkh"), (wvh, wv, "wvh"), (wqh, wq, "wqh")):
                            srcw = wsrc[:, :, h0:h0 + 512]
                            dstw = wdst.ap().rearrange("p (c n) -> p c n", c=8)
                            em.op("pool", DMA(dstw, srcw), writes=[wkey], dma=wkey)

                        for sc in range(NSC):
                            b = sc % 2
                            xv = xfm.ap()[:, b * 4096:(b + 1) * 4096]
                            src3 = xfmd[s].rearrange("(c p) n -> p c n", p=128)[:, :, sc * 512:(sc + 1) * 512]
                            em.op("pool", DMA(xv.rearrange("p (c n) -> p c n", n=512), src3),
                                  writes=[f"xfm{b}"], dma=f"xfm{b}")
                            for c in range(4):
                                reg = (psA if c % 2 == 0 else psB).ap()[:, 0:512]
                                bk = f"psSb.{c % 2}"
                                for dinc in range(8):
                                    em.op("pe",
                                          MM(reg,
                                             wkh.ap()[:, dinc * 512 + c * 128: dinc * 512 + (c + 1) * 128],
                                             xv[:, dinc * 512:(dinc + 1) * 512],
                                             dinc == 0, dinc == 7),
                                          reads=["wkh", f"xfm{b}"], writes=[bk])
                                em.op("act",
                                      ACOPY(Kh.ap()[:, c * S + sc * 512: c * S + (sc + 1) * 512], reg),
                                      reads=[bk], writes=["K"])
                            for t in range(4):
                                kc = sc * 4 + t
                                reg = psD.ap()[:, (t % 2) * 512:((t % 2) + 1) * 512]
                                bk = f"psO.{t % 2}"
                                for dinc in range(8):
                                    em.op("pe",
                                          MM(reg,
                                             xv[:, dinc * 512 + t * 128: dinc * 512 + t * 128 + 128],
                                             wvh.ap()[:, dinc * 512:(dinc + 1) * 512],
                                             dinc == 0, dinc == 7),
                                          reads=["wvh", f"xfm{b}"], writes=[bk])
                                vdst = Va.ap()[:, kc * 528: kc * 528 + 528].rearrange(
                                    "p (h d) -> p h d", h=8)[:, :, 0:64]
                                vsrc = reg.rearrange("p (h d) -> p h d", h=8)
                                em.op("dve", VCOPY(vdst, vsrc), reads=[bk], writes=["V"])

                        for qt in range(NQT):
                            t0 = qt * QT
                            xv = xfm.ap()[:, 0:8 * QT]
                            src3 = xfmd[s].rearrange("(c p) n -> p c n", p=128)[:, :, t0:t0 + QT]
                            em.op("pool", DMA(xv.rearrange("p (c n) -> p c n", n=QT), src3),
                                  writes=["xfm0", "xfm1"], dma="xfm0")
                            for c in range(4):
                                for nh in range(QT // 512):
                                    bk2 = "psSb.2"
                                    reg = psC.ap()[:, (nh % 2) * 512: (nh % 2) * 512 + 512]
                                    for dinc in range(8):
                                        em.op("pe",
                                              MM(reg,
                                                 wqh.ap()[:, dinc * 512 + c * 128: dinc * 512 + (c + 1) * 128],
                                                 xv[:, dinc * QT + nh * 512:dinc * QT + (nh + 1) * 512],
                                                 dinc == 0, dinc == 7),
                                              reads=["wqh", "xfm0", "xfm1"], writes=[bk2])
                                    em.op("dve",
                                          VCOPY(qp.ap()[0:64, (2 * c) * QT + nh * 512:(2 * c) * QT + nh * 512 + 512],
                                                reg[0:64, :]),
                                          reads=[bk2], writes=[f"qp.{2 * c}"])
                                    em.op("dve",
                                          VCOPY(qp.ap()[64:128, (2 * c + 1) * QT + nh * 512:(2 * c + 1) * QT + nh * 512 + 512],
                                                reg[64:128, :]),
                                          reads=[bk2], writes=[f"qp.{2 * c + 1}"])
                            pending_norm = []
                            for hh in range(8):
                                hg = half * 8 + hh
                                for kc in range(KC + 2):
                                    if kc < KC:
                                        sl = kc % 3
                                        for qh in range(QT // 512):
                                            em.op("pe",
                                                  MM([psA, psB, psC][sl].ap()[:, qh * 512:(qh + 1) * 512],
                                                     Kh.ap()[:, (hh // 2) * S + kc * 128:(hh // 2) * S + (kc + 1) * 128],
                                                     qp.ap()[:, hh * QT + qh * 512:hh * QT + (qh + 1) * 512],
                                                     True, True),
                                                  reads=["K", f"qp.{hh}"], writes=[f"psSb.{sl}"])
                                        em.op("act",
                                              EXPA(pbuf.ap()[:, sl * QT:(sl + 1) * QT],
                                                   [psA, psB, psC][sl].ap()[:, 0:QT],
                                                   cbi[:, 0:1], csc[:, 0:1]),
                                              reads=[f"psSb.{sl}", "csc", "cbi"],
                                              writes=[f"P{sl}"])
                                    if kc >= 2:
                                        kp = kc - 2
                                        slp = kp % 3
                                        for qh in range(QT // 512):
                                            em.op("pe",
                                                  MM(psD.ap()[0:65, qh * 512:(qh + 1) * 512],
                                                     Va.ap()[:, kp * 528 + hh * 66: kp * 528 + hh * 66 + 65],
                                                     pbuf.ap()[:, slp * QT + qh * 512: slp * QT + (qh + 1) * 512],
                                                     kp == 0, kp == KC - 1),
                                                  reads=[f"P{slp}", "V"], writes=[f"psO.{qh}"])
                                    if kc == min(8, KC) and pending_norm:
                                        pending_norm.pop()()
                                em.op("dve", VCOPY(obuf[:], psD[0:65, 0:QT]),
                                      reads=["psO.0", "psO.1"], writes=["obuf"])
                                def _recip(e):
                                    with nc.allow_low_precision(reason="1/l feeds a bf16 matmul operand"):
                                        return e.reciprocal(rlb[0:1, 0:QT], obuf[64:65, 0:QT])
                                em.op("dve", _recip, reads=["obuf"], writes=["rlb"])

                                def mk_norm(hg=hg, t0=t0):
                                    def go():
                                        for qh in range(QT // 512):
                                            em.op("pe",
                                                  MM(psC.ap()[0:128, qh * 512:(qh + 1) * 512],
                                                     ones64[0:1, 0:128],
                                                     rlb[0:1, qh * 512:(qh + 1) * 512], True, True),
                                                  reads=["rlb", "ones64"],
                                                  writes=["psSb.2"])
                                        em.op("dve", VCOPY(bcs[:], psC[0:64, 0:QT]),
                                              reads=["psSb.2"], writes=["bcs"])
                                        arow = (hg % 2) * 64
                                        acol = (hg // 2) * S + t0
                                        em.op("dve",
                                              TT(afm.ap()[arow:arow + 64, acol:acol + QT],
                                                 obuf[0:64, 0:QT], bcs[:], OP.mult),
                                              reads=["obuf", "bcs"], writes=["attnfm"])
                                    return go
                                pending_norm.append(mk_norm())
                            if pending_norm:
                                pending_norm.pop()()
                    if DEBUG and s == 0:
                        em.op("sp", DMA(dbg["K"], Kh[:]), reads=["K"], dma="dbgK")
                        em.op("sp", DMA(dbg["V"], Va[:]), reads=["V"], dma="dbgV")
                        em.op("sp", DMA(dbg["qp"], qp[:]), reads=[f"qp.{i}" for i in range(8)], dma="dbgqp")
                    em.barrier()
                if DEBUG and s == 0:
                    em.op("sp", DMA(dbg["afm"], afm[:]), reads=["attnfm"], dma="dbgafm")

                with ExitStack() as ph:
                    EF = ph.enter_context
                    x1 = EF(nc.sbuf_tensor([128, TSUBS * D], F32))
                    x1tmb = EF(nc.sbuf_tensor([128, D], BF16))
                    x1fm = EF(nc.sbuf_tensor([128, 8 * FT], BF16))
                    yb = EF(nc.sbuf_tensor([128, TSUBS * D], F32))
                    hbuf = EF(nc.sbuf_tensor([128, 8 * FT], BF16))
                    ws = [EF(nc.sbuf_tensor(f"wstream_s{s}_{i}", [128, 8 * 512], BF16)) for i in range(3)]
                    xres = EF(nc.sbuf_tensor([128, 2 * D], F32))
                    psU, psS1, psS2, psV = psA, psB, psC, psD

                    ca_ones = combaug.ap().rearrange("p (t n) -> p t n", n=9)[:, :, 8:9]
                    em.op("dve", (lambda a=ca_ones: (lambda e: e.memset(a, 1.0)))(), writes=["combaug"])

                    wsctr = [0]

                    def stream(src_ap, key):
                        i = wsctr[0] % 3
                        wsctr[0] += 1
                        dst = ws[i].ap().rearrange("p (c n) -> p c n", c=8)
                        em.op("pool", DMA(dst, src_ap), writes=[f"ws{i}"], dma=f"ws{i}")
                        return ws[i], f"ws{i}"

                    def ln_batch():
                        em.op("dve", TS(gmean[:], zsum[:], 1.0 / D, None, OP.mult),
                              reads=["zsum"], writes=["gmean"])
                        em.op("dve", TT(gtmp[:], gmean[:], gmean[:], OP.mult),
                              reads=["gmean"], writes=["gtmp"])
                        em.op("dve", STT(gsc1[:], zsq[:], 1.0 / D, gtmp[:], OP.mult, OP.subtract),
                              reads=["zsq", "gtmp"], writes=["gsc1"])
                        em.op("act", (lambda e: e.activation(gtmp[:], gsc1[:], AF.Ln, bias=ceps[:, 0:1])),
                              reads=["gsc1", "ceps"], writes=["gtmp"])
                        em.op("dve", TS(gtmp[:], gtmp[:], -0.5, None, OP.mult),
                              reads=["gtmp"], writes=["gtmp"])
                        em.op("act", (lambda e: e.activation(grstd[:], gtmp[:], AF.Exp)),
                              reads=["gtmp"], writes=["grstd"])

                    for f in range(FCH):
                        wop = [stream(wo[:, :, dh * 512:(dh + 1) * 512], f"s{s}wo{f}.{dh}")
                               for dh in range(2)]
                        for ts in range(TSUBS):
                            tok0 = f * FT + ts * 128
                            xr = xres.ap()[:, (ts % 2) * D:((ts % 2) + 1) * D]
                            em.op("sp", DMA(xr, xtm[s, tok0:tok0 + 128, :]),
                                  writes=[f"xres{ts % 2}"], dma=f"xres{ts % 2}")
                            pw, pwk = (psU, "psU") if ts % 2 == 0 else (psS1, "psS1")
                            for dh in range(2):
                                wbuf, wkey = wop[dh]
                                for dinc in range(8):
                                    em.op("pe",
                                          MM(pw.ap()[:, dh * 512:(dh + 1) * 512],
                                             afm.ap()[:, dinc * S + tok0: dinc * S + tok0 + 128],
                                             wbuf.ap()[:, dinc * 512:(dinc + 1) * 512],
                                             dinc == 0, dinc == 7),
                                          reads=["attnfm", wkey], writes=[f"{pwk}.{dh}"])
                            x1s = x1.ap()[:, ts * D:(ts + 1) * D]
                            em.op("dve",
                                  STT(x1s, pw.ap()[:, 0:1024], 1.0, xr, OP.mult, OP.add,
                                      zsum[:, ts:ts + 1]),
                                  reads=[f"{pwk}.0", f"{pwk}.1", f"xres{ts % 2}"],
                                  writes=[f"x1.{ts}", "zsum"])
                            em.op("dve",
                                  STT(xr, x1s, 1.0, x1s, OP.mult, OP.mult, zsq[:, ts:ts + 1]),
                                  reads=[f"x1.{ts}"], writes=[f"xres{ts % 2}", "zsq"])
                        ln_batch()
                        for ts in range(TSUBS):
                            x1s = x1.ap()[:, ts * D:(ts + 1) * D]
                            em.op("dve",
                                  TS(x1s, x1s, gmean[:, ts:ts + 1], grstd[:, ts:ts + 1],
                                     OP.subtract, OP.mult),
                                  reads=["gmean", "grstd", f"x1.{ts}"], writes=[f"x1.{ts}"])
                            em.op("dve", VCOPY(x1tmb[:], x1s), reads=[f"x1.{ts}"], writes=["x1tmb"])
                            for dinc in range(8):
                                em.op("pe",
                                      MM(psV.ap()[:, dinc * 128:(dinc + 1) * 128],
                                         x1tmb.ap()[:, dinc * 128:(dinc + 1) * 128],
                                         identb[:], True, True),
                                      reads=["x1tmb", "ident"], writes=[f"psV.{dinc // 4}"])
                            fmdst = x1fm.ap().rearrange("p (c t) -> p c t", c=8)[:, :, ts * 128:(ts + 1) * 128]
                            fmsrc = psV.ap().rearrange("p (c t) -> p c t", c=8)
                            em.op("dve", VCOPY(fmdst, fmsrc),
                                  reads=["psV.0", "psV.1"], writes=["x1fm"])
                        if DEBUG and s == 0 and f == 0:
                            em.op("sp", DMA(dbg["x1"], x1[:]), reads=[f"x1.{t2}" for t2 in range(TSUBS)], dma="dbgx1")
                        for ts in range(TSUBS):
                            for dinc in range(8):
                                em.op("pe",
                                      MM(psS2.ap()[:, ts * 8:(ts + 1) * 8],
                                         x1fm.ap()[:, dinc * FT + ts * 128: dinc * FT + (ts + 1) * 128],
                                         gfmt.ap()[:, dinc * 8:(dinc + 1) * 8],
                                         dinc == 0, dinc == 7),
                                      reads=["x1fm", "gfmt"], writes=["psS2.0"])
                        em.op("act",
                              (lambda e: e.activation(scores[:, 0:8 * TSUBS], psS2[:, 0:8 * TSUBS], AF.Exp)),
                              reads=["psS2.0"], writes=["scores"])
                        for ts in range(TSUBS):
                            sl = scores.ap()[:, ts * 8:(ts + 1) * 8]
                            m1 = gm1[:, ts:ts + 1]
                            m2 = gm2[:, ts:ts + 1]
                            c1 = gsc1[:, ts:ts + 1]
                            t1 = gtmp[:, ts:ts + 1]
                            em.op("dve", RED(m1, sl, "max"), reads=["scores"], writes=["gm1"])
                            em.op("dve", TS(geq[:], sl, m1, None, OP.is_ge),
                                  reads=["scores", "gm1"], writes=["geq"])
                            em.op("dve", RED(c1, geq[:], "sum"), reads=["geq"], writes=["gsc1"])
                            em.op("dve", STT(gmask[:], geq[:], -1e30, sl, OP.mult, OP.add),
                                  reads=["geq", "scores"], writes=["gmask"])
                            em.op("dve", RED(m2, gmask[:], "max"), reads=["gmask"], writes=["gm2"])
                            em.op("dve", TS(t1, c1, 2.0, None, OP.is_ge),
                                  reads=["gsc1"], writes=["gtmp"])
                            em.op("dve", TT(c1, m1, m2, OP.subtract),
                                  reads=["gm1", "gm2"], writes=["gsc1"])
                            em.op("dve", STT(m2, t1, c1, m2, OP.mult, OP.add),
                                  reads=["gtmp", "gsc1", "gm2"], writes=["gm2"])
                            em.op("dve", TS(geq[:], sl, m2, None, OP.is_ge),
                                  reads=["scores", "gm2"], writes=["geq"])
                            em.op("dve", TT(gmask[:], sl, geq[:], OP.mult),
                                  reads=["scores", "geq"], writes=["gmask"])
                            em.op("dve", TT(t1, m1, m2, OP.add),
                                  reads=["gm1", "gm2"], writes=["gtmp"])
                            em.op("dve", (lambda o, i: (lambda e: e.reciprocal(o, i)))(c1, t1),
                                  reads=["gtmp"], writes=["gsc1"])
                            em.op("dve", TS(comb.ap()[:, ts * 8:(ts + 1) * 8], gmask[:], c1, None, OP.mult),
                                  reads=["gmask", "gsc1"], writes=["comb"])
                            em.op("dve", VCOPY(combaug[:, ts * 9:ts * 9 + 8], comb.ap()[:, ts * 8:(ts + 1) * 8]),
                                  reads=["comb"], writes=["combaug"])
                        if DEBUG and s == 0 and f == 0:
                            em.op("sp", DMA(dbg["scores"], scores[:]), reads=["scores"], dma="dbgsc")
                            em.op("sp", DMA(dbg["comb"], comb[:]), reads=["comb"], dma="dbgcb")
                        for e9 in [8] + list(range(8)):
                            for hh2 in range(2):
                                ubuf, ukey = stream(upw[e9][:, :, hh2 * 512:(hh2 + 1) * 512],
                                                    f"u{e9}.{hh2}")
                                for hcl in range(4):
                                    hc = hh2 * 4 + hcl
                                    pw, pwk = (psU, "psU") if hc % 2 == 0 else (psS1, "psS1")
                                    for th in range(FT // 512):
                                        for dinc in range(8):
                                            em.op("pe",
                                                  MM(pw.ap()[:, th * 512:(th + 1) * 512],
                                                     ubuf.ap()[:, dinc * 512 + hcl * 128: dinc * 512 + (hcl + 1) * 128],
                                                     x1fm.ap()[:, dinc * FT + th * 512: dinc * FT + (th + 1) * 512],
                                                     dinc == 0, dinc == 7),
                                                  reads=[ukey, "x1fm"], writes=[f"{pwk}.{th}"])
                                    em.op("act",
                                          (lambda d, srcap, bap: (lambda e: e.activation(
                                              d, srcap, AF.Silu, bias=bap)))(
                                              hbuf.ap()[:, hc * FT:(hc + 1) * FT],
                                              pw.ap()[:, 0:FT],
                                              ubfm.ap()[:, e9 * 8 + hc: e9 * 8 + hc + 1]),
                                          reads=[f"{pwk}.0"] + ([f"{pwk}.1"] if FT > 512 else []) + ["ubfm"],
                                          writes=[f"h.{hc}"])
                            if e9 == 8:
                                for ts in range(TSUBS):
                                    em.op("pe", MM(psV.ap()[0:NEXP, ts * 128:(ts + 1) * 128],
                                                   combaug.ap()[:, ts * 9:(ts + 1) * 9], identb[:], True, True),
                                          reads=["combaug", "ident"],
                                          writes=[f"psV.{(ts * 128) // 512}"])
                                em.op("dve", VCOPY(combT[:], psV[0:NEXP, 0:128 * TSUBS]),
                                      reads=["psV.0", "psV.1"], writes=["combT"])
                            dnp = [stream(dnw[e9][:, :, dh * 512:(dh + 1) * 512], f"d{e9}.{dh}")
                                   for dh in range(2)]
                            for ts in range(TSUBS):
                                pd, pdk = (psS2, "psS2") if ts % 2 == 0 else (psV, "psV")
                                for dh in range(2):
                                    dbuf, dkey = dnp[dh]
                                    for hc in range(8):
                                        em.op("pe",
                                              MM(pd.ap()[:, dh * 512:(dh + 1) * 512],
                                                 hbuf.ap()[:, hc * FT + ts * 128: hc * FT + (ts + 1) * 128],
                                                 dbuf.ap()[:, hc * 512:(hc + 1) * 512],
                                                 hc == 0, (hc == 7 and e9 != 8)),
                                              reads=[dkey, f"h.{hc}"], writes=[f"{pdk}.{dh}"])
                                    if e9 == 8:
                                        em.op("pe",
                                              MM(pd.ap()[:, dh * 512:(dh + 1) * 512],
                                                 combT.ap()[0:NEXP, ts * 128:(ts + 1) * 128],
                                                 db9.ap()[:, dh * 512:(dh + 1) * 512],
                                                 False, True),
                                              reads=["combT", "db9"], writes=[f"{pdk}.{dh}"])
                                ysl = yb.ap()[:, ts * D:(ts + 1) * D]
                                if e9 == 8:
                                    em.op("dve", VCOPY(ysl, pd[:, 0:1024]),
                                          reads=[f"{pdk}.0", f"{pdk}.1"], writes=[f"y.{ts}"])
                                else:
                                    cs = comb.ap()[:, ts * 8 + e9: ts * 8 + e9 + 1]
                                    em.op("dve",
                                          STT(ysl, pd.ap()[:, 0:1024], cs, ysl, OP.mult, OP.add),
                                          reads=[f"{pdk}.0", f"{pdk}.1", "comb", f"y.{ts}"], writes=[f"y.{ts}"])
                        if DEBUG and s == 0 and f == 0:
                            em.op("sp", DMA(dbg["y"], yb[:]), reads=[f"y.{t2}" for t2 in range(TSUBS)], dma="dbgy")
                        for ts in range(TSUBS):
                            x1s = x1.ap()[:, ts * D:(ts + 1) * D]
                            ysl = yb.ap()[:, ts * D:(ts + 1) * D]
                            em.op("dve",
                                  STT(x1s, ysl, 1.0, x1s, OP.mult, OP.add, zsum[:, ts:ts + 1]),
                                  reads=[f"y.{ts}", f"x1.{ts}"], writes=[f"x1.{ts}", "zsum"])
                            em.op("dve",
                                  STT(xres.ap()[:, (ts % 2) * D:((ts % 2) + 1) * D], x1s, 1.0, x1s,
                                      OP.mult, OP.mult, zsq[:, ts:ts + 1]),
                                  reads=[f"x1.{ts}"], writes=[f"xres{ts % 2}", "zsq"])
                        ln_batch()
                        for ts in range(TSUBS):
                            tok0 = f * FT + ts * 128
                            x1s = x1.ap()[:, ts * D:(ts + 1) * D]
                            em.op("dve",
                                  TS(x1s, x1s, gmean[:, ts:ts + 1], grstd[:, ts:ts + 1],
                                     OP.subtract, OP.mult),
                                  reads=["gmean", "grstd", f"x1.{ts}"], writes=[f"x1.{ts}"])
                            em.op("sp", DMA(outd[s, tok0:tok0 + 128, :], x1s),
                                  reads=[f"x1.{ts}"], dma="out")
                    em.barrier()

        with nc.Block() as block:
            em.replay(block)
    return nc


def _tile_w(w):
    return np.ascontiguousarray(w.reshape(8, 128, D).transpose(1, 0, 2))


def kernel(**inp):
    x = np.ascontiguousarray(np.asarray(inp["L0_attn_x"], dtype=np.float32))
    B, S, _ = x.shape
    wqh = _tile_w(np.asarray(inp["L0_attn_w_q"], dtype=np.float32))
    wkh = _tile_w(np.asarray(inp["L0_attn_w_k"], dtype=np.float32))
    wvh = _tile_w(np.asarray(inp["L0_attn_w_v"], dtype=np.float32))
    woh = _tile_w(np.asarray(inp["L0_attn_w_o"], dtype=np.float32))
    upw9 = np.concatenate([np.asarray(inp["L0_ffn_up_w"], dtype=np.float32),
                           np.asarray(inp["L0_ffn_sh_up_w"], dtype=np.float32)[None]], axis=0)
    dnw9 = np.concatenate([np.asarray(inp["L0_ffn_down_w"], dtype=np.float32),
                           np.asarray(inp["L0_ffn_sh_down_w"], dtype=np.float32)[None]], axis=0)
    upw_t = np.ascontiguousarray(upw9.reshape(NEXP, 8, 128, D).transpose(0, 2, 1, 3))
    dnw_t = np.ascontiguousarray(dnw9.reshape(NEXP, 8, 128, D).transpose(0, 2, 1, 3))
    upb9 = np.concatenate([np.asarray(inp["L0_ffn_up_b"], dtype=np.float32),
                           np.asarray(inp["L0_ffn_sh_up_b"], dtype=np.float32)[None]], axis=0)
    upbfm = np.ascontiguousarray(upb9.reshape(NEXP, 8, 128).transpose(2, 0, 1).reshape(128, 72))
    db9 = np.ascontiguousarray(np.concatenate(
        [np.asarray(inp["L0_ffn_down_b"], dtype=np.float32),
         np.asarray(inp["L0_ffn_sh_down_b"], dtype=np.float32)[None]], axis=0))
    gate = np.asarray(inp["L0_ffn_gate_w"], dtype=np.float32)
    gatefmt = np.ascontiguousarray(gate.reshape(8, 8, 128).transpose(2, 1, 0).reshape(128, 64))
    ident = np.eye(128, dtype=ml_dtypes.bfloat16)

    BC = B // 8
    nc = build_program(S=S, BC=BC, FT=1024)
    in_maps = []
    for c in range(8):
        xc = np.ascontiguousarray(x[c * BC:(c + 1) * BC])
        xfmc = np.ascontiguousarray(xc.transpose(0, 2, 1))
        in_maps.append(dict(x_tm=xc, x_fm=xfmc, wq=wqh, wk=wkh, wv=wvh, wo=woh,
                            upw=upw_t, dnw=dnw_t, gatefmt=gatefmt, upbfm=upbfm,
                            downb9=db9, ident=ident))
    res = run_bass_kernel_spmd(nc, in_maps, core_ids=list(range(8)))
    out = np.concatenate([res.results[i]["out"] for i in range(8)], axis=0)
    return out


# version 25
# speedup vs baseline: 1.0032x; 1.0032x over previous
import numpy as np
import ml_dtypes
from contextlib import ExitStack
import concourse.bass as bass
import concourse.mybir as mybir
from concourse.bass_utils import run_bass_kernel_spmd

F32 = mybir.dt.float32
BF16 = mybir.dt.bfloat16
AF = mybir.ActivationFunctionType
OP = mybir.AluOpType
AXX = mybir.AxisListType.X

D = 1024
NEXP = 9
ATTN_C = 4.0

QUEUES = ("pe", "act", "dve", "pool", "sp")
SEMS = ("pe", "act", "dve", "pool", "swd", "hwd")


class Em:

    def __init__(self, nc):
        self.nc = nc
        self.semh = {}
        self.cnt = {}
        self.q = {qn: [] for qn in QUEUES}
        self.hwm = {qn: {} for qn in QUEUES}
        self.res = {}
        self.nops = 0
        self.stack = None

    def sem(self, name):
        if name not in self.semh:
            self.semh[name] = self.stack.enter_context(self.nc.semaphore(name))
            self.cnt[name] = 0
        return name

    def _wait(self, qn, sem, val):
        if val <= 0 or self.hwm[qn].get(sem, 0) >= val:
            return
        self.hwm[qn][sem] = val
        self.q[qn].append(("w", sem, val))

    def _need(self, qn, producer):
        sem, val = producer
        if qn == "pe" and sem == "pe":
            return
        self._wait(qn, sem, val)

    def op(self, qn, fn, reads=(), writes=(), dma=False):
        if dma:
            assert isinstance(dma, str)
            sem = self.sem("D." + dma)
        else:
            sem = self.sem(qn)
        for k in reads:
            r = self.res.get(k)
            if r and r["w"] is not None:
                self._need(qn, r["w"])
        for k in writes:
            r = self.res.get(k)
            if r:
                if r["w"] is not None:
                    self._need(qn, r["w"])
                for rs, rv in r["r"].items():
                    self._need(qn, (rs, rv))
        inc = 16 if dma else 1
        self.cnt[sem] += inc
        tok = (sem, self.cnt[sem])
        self.q[qn].append(("o", fn, sem, inc))
        self.nops += 1
        for k in reads:
            r = self.res.setdefault(k, {"w": None, "r": {}})
            r["r"][sem] = max(r["r"].get(sem, 0), tok[1])
        for k in writes:
            self.res[k] = {"w": tok, "r": {}}
        return tok

    def barrier(self):
        snap = dict(self.cnt)
        for qn in QUEUES:
            for sname, v in snap.items():
                self._wait(qn, sname, v)

    def replay(self, block):
        mapping = {"pe": block.tensor, "act": block.scalar, "dve": block.vector,
                   "pool": block.gpsimd, "sp": block.sync}

        def mk(qn):
            def body(eng):
                for e in self.q[qn]:
                    if e[0] == "w":
                        eng.wait_ge(self.semh[e[1]], e[2])
                    else:
                        _, fn, sem, inc = e
                        fn(eng).then_inc(self.semh[sem], inc)
            return body
        for qn in QUEUES:
            mapping[qn](mk(qn))


def MM(o, l, r, st, sp):
    return lambda e: e.matmul(o, l, r, start=st, stop=sp)


def DMA(d, srcap):
    return lambda e: e.dma_start(out=d, in_=srcap)


def ACOPY(d, srcap):
    return lambda e: e.copy(d, srcap)


def VCOPY(d, srcap):
    return lambda e: e.tensor_copy(d, srcap)


def EXPA(d, srcap, biasap, scaleap):
    return lambda e: e.activation(d, srcap, AF.Exp, bias=biasap, scale=scaleap)


def STT(o, i0, sc, i1, op0, op1, acc=None):
    return lambda e: e.scalar_tensor_tensor(out=o, in0=i0, scalar=sc, in1=i1,
                                            op0=op0, op1=op1, accum_out=acc)


def TS(o, i0, s1, s2, op0, op1=None):
    if op1 is None:
        return lambda e: e.tensor_scalar(out=o, in0=i0, scalar1=s1, scalar2=None, op0=op0)
    return lambda e: e.tensor_scalar(out=o, in0=i0, scalar1=s1, scalar2=s2, op0=op0, op1=op1)


def TT(o, a, b, opx):
    return lambda e: e.tensor_tensor(o, a, b, opx)


def RED(o, i, opx):
    if opx == "max":
        return lambda e: e.reduce_max(o, i, axis=AXX)
    return lambda e: e.reduce_sum(o, i, axis=AXX)


def build_program(S=4096, BC=2, FT=1024, DEBUG=False):
    KC = S // 128
    QT = min(1024, S)
    NQT = S // QT
    NSC = S // 512
    FCH = S // FT
    TSUBS = FT // 128

    nc = bass.Bass("TRN2", target_bir_lowering=False)
    xtm = nc.dram_tensor("x_tm", [BC, S, D], F32, kind="ExternalInput").ap()
    xfmd = nc.dram_tensor("x_fm", [BC, D, S], F32, kind="ExternalInput").ap()
    wq = nc.dram_tensor("wq", [128, 8, D], F32, kind="ExternalInput").ap()
    wk = nc.dram_tensor("wk", [128, 8, D], F32, kind="ExternalInput").ap()
    wv = nc.dram_tensor("wv", [128, 8, D], F32, kind="ExternalInput").ap()
    wo = nc.dram_tensor("wo", [128, 8, D], F32, kind="ExternalInput").ap()
    upw = nc.dram_tensor("upw", [NEXP, 128, 8, D], F32, kind="ExternalInput").ap()
    dnw = nc.dram_tensor("dnw", [NEXP, 128, 8, D], F32, kind="ExternalInput").ap()
    gfm = nc.dram_tensor("gatefmt", [128, 64], F32, kind="ExternalInput").ap()
    ubf = nc.dram_tensor("upbfm", [128, 72], F32, kind="ExternalInput").ap()
    db9d = nc.dram_tensor("downb9", [NEXP, D], F32, kind="ExternalInput").ap()
    identd = nc.dram_tensor("ident", [128, 128], BF16, kind="ExternalInput").ap()
    outd = nc.dram_tensor("out", [BC, S, D], F32, kind="ExternalOutput").ap()
    dbg = {}
    if DEBUG:
        dbg["afm"] = nc.dram_tensor("dbg_afm", [128, 8 * S], BF16, kind="ExternalOutput").ap()
        dbg["x1"] = nc.dram_tensor("dbg_x1", [128, (S // FT) * FT // 128 * D], F32, kind="ExternalOutput").ap()
        dbg["scores"] = nc.dram_tensor("dbg_scores", [128, 8 * (FT // 128)], F32, kind="ExternalOutput").ap()
        dbg["comb"] = nc.dram_tensor("dbg_comb", [128, 8 * (FT // 128)], F32, kind="ExternalOutput").ap()
        dbg["y"] = nc.dram_tensor("dbg_y", [128, (FT // 128) * D], F32, kind="ExternalOutput").ap()
        dbg["K"] = nc.dram_tensor("dbg_K", [128, 4 * S], BF16, kind="ExternalOutput").ap()
        dbg["V"] = nc.dram_tensor("dbg_V", [128, (S // 128) * 528], BF16, kind="ExternalOutput").ap()
        dbg["qp"] = nc.dram_tensor("dbg_qp", [128, 8 * 512], BF16, kind="ExternalOutput").ap()

    em = Em(nc)

    with ExitStack() as top:
        E = top.enter_context
        em.stack = top
        identb = E(nc.sbuf_tensor([128, 128], BF16))
        gfmt = E(nc.sbuf_tensor([128, 64], BF16))
        ubfm = E(nc.sbuf_tensor([128, 72], F32))
        db9 = E(nc.sbuf_tensor([NEXP, D], BF16))
        csc = E(nc.sbuf_tensor([128, 1], F32))
        cbi = E(nc.sbuf_tensor([128, 1], F32))
        ceps = E(nc.sbuf_tensor([128, 1], F32))
        ones64 = E(nc.sbuf_tensor([1, 128], BF16))
        ones11 = E(nc.sbuf_tensor([128, 1], F32))
        obuf = E(nc.sbuf_tensor([65, QT], BF16))
        bcs = E(nc.sbuf_tensor([64, QT], BF16))
        rlb = E(nc.sbuf_tensor([1, QT], BF16))
        TSUBS0 = FT // 128
        psA = E(nc.psum_tensor([128, 1024], F32))
        psB = E(nc.psum_tensor([128, 1024], F32))
        psC = E(nc.psum_tensor([128, 1024], F32))
        psD = E(nc.psum_tensor([128, 1024], F32))
        scores = E(nc.sbuf_tensor([128, 8 * TSUBS0], F32))
        comb = E(nc.sbuf_tensor([128, 8 * TSUBS0], F32))
        combaug = E(nc.sbuf_tensor([128, 9 * TSUBS0], BF16))
        combT = E(nc.sbuf_tensor([NEXP, 128 * TSUBS0], BF16))
        geq = E(nc.sbuf_tensor([128, 8], F32))
        gmask = E(nc.sbuf_tensor([128, 8], F32))
        zsum = E(nc.sbuf_tensor([128, TSUBS0], F32))
        zsq = E(nc.sbuf_tensor([128, TSUBS0], F32))
        gmean = E(nc.sbuf_tensor([128, TSUBS0], F32))
        grstd = E(nc.sbuf_tensor([128, TSUBS0], F32))
        gtmp = E(nc.sbuf_tensor([128, TSUBS0], F32))
        gm1 = E(nc.sbuf_tensor([128, TSUBS0], F32))
        gm2 = E(nc.sbuf_tensor([128, TSUBS0], F32))
        gsc1 = E(nc.sbuf_tensor([128, TSUBS0], F32))
        em.op("sp", DMA(identb[:], identd), writes=["ident"], dma="ident")
        em.op("pool", DMA(gfmt[:], gfm), writes=["gfmt"], dma="gfmt")
        em.op("sp", DMA(ubfm[:], ubf), writes=["ubfm"], dma="ubfm")
        em.op("pool", DMA(db9[:], db9d), writes=["db9"], dma="db9")
        em.op("dve", (lambda e: e.memset(csc[:], 0.125)), writes=["csc"])
        em.op("dve", (lambda e: e.memset(cbi[:], -ATTN_C)), writes=["cbi"])
        em.op("dve", (lambda e: e.memset(ceps[:], 1e-5)), writes=["ceps"])
        em.op("dve", (lambda e: e.memset(ones64[:], 1.0)), writes=["ones64"])
        em.op("dve", (lambda e: e.memset(ones11[:], 1.0)), writes=["ones11"])

        for s in range(BC):
            with ExitStack() as seqscope:
                afm = seqscope.enter_context(nc.sbuf_tensor([128, 8 * S], BF16))

                with ExitStack() as ph:
                    EA = ph.enter_context
                    Kh = EA(nc.sbuf_tensor([128, 4 * S], BF16))
                    Va = EA(nc.sbuf_tensor([128, KC * 528], BF16))
                    wkh = EA(nc.sbuf_tensor([128, 8 * 512], BF16))
                    wvh = EA(nc.sbuf_tensor([128, 8 * 512], BF16))
                    wqh = EA(nc.sbuf_tensor([128, 8 * 512], BF16))
                    xfm = EA(nc.sbuf_tensor([128, 8192], BF16))
                    qp = EA(nc.sbuf_tensor([128, 8 * QT], BF16))
                    pbuf = EA(nc.sbuf_tensor([128, 3 * QT], BF16))



                    em.op("dve", (lambda q=qp: (lambda e: e.memset(q[:], 0.0)))(),
                          writes=[f"qp.{i}" for i in range(8)])
                    em.op("dve", (lambda q=Va: (lambda e: e.memset(q[:], 0.0)))(), writes=["V"])
                    va_ones = Va.ap().rearrange("p (k d) -> p k d", d=66)[:, :, 64:65]
                    em.op("dve", (lambda a=va_ones: (lambda e: e.memset(a, 1.0)))(), writes=["V"])

                    for half in range(2):
                        h0 = half * 512
                        for (wdst, wsrc, wkey) in ((wkh, wk, "wkh"), (wvh, wv, "wvh"), (wqh, wq, "wqh")):
                            srcw = wsrc[:, :, h0:h0 + 512]
                            dstw = wdst.ap().rearrange("p (c n) -> p c n", c=8)
                            em.op("pool", DMA(dstw, srcw), writes=[wkey], dma=wkey)

                        for sc in range(NSC):
                            b = sc % 2
                            xv = xfm.ap()[:, b * 4096:(b + 1) * 4096]
                            src3 = xfmd[s].rearrange("(c p) n -> p c n", p=128)[:, :, sc * 512:(sc + 1) * 512]
                            em.op("pool", DMA(xv.rearrange("p (c n) -> p c n", n=512), src3),
                                  writes=[f"xfm{b}"], dma=f"xfm{b}")
                            for c in range(4):
                                reg = (psA if c % 2 == 0 else psB).ap()[:, 0:512]
                                bk = f"psSb.{c % 2}"
                                for dinc in range(8):
                                    em.op("pe",
                                          MM(reg,
                                             wkh.ap()[:, dinc * 512 + c * 128: dinc * 512 + (c + 1) * 128],
                                             xv[:, dinc * 512:(dinc + 1) * 512],
                                             dinc == 0, dinc == 7),
                                          reads=["wkh", f"xfm{b}"], writes=[bk])
                                em.op("act",
                                      ACOPY(Kh.ap()[:, c * S + sc * 512: c * S + (sc + 1) * 512], reg),
                                      reads=[bk], writes=["K"])
                            for t in range(4):
                                kc = sc * 4 + t
                                reg = psD.ap()[:, (t % 2) * 512:((t % 2) + 1) * 512]
                                bk = f"psO.{t % 2}"
                                for dinc in range(8):
                                    em.op("pe",
                                          MM(reg,
                                             xv[:, dinc * 512 + t * 128: dinc * 512 + t * 128 + 128],
                                             wvh.ap()[:, dinc * 512:(dinc + 1) * 512],
                                             dinc == 0, dinc == 7),
                                          reads=["wvh", f"xfm{b}"], writes=[bk])
                                vdst = Va.ap()[:, kc * 528: kc * 528 + 528].rearrange(
                                    "p (h d) -> p h d", h=8)[:, :, 0:64]
                                vsrc = reg.rearrange("p (h d) -> p h d", h=8)
                                em.op("dve", VCOPY(vdst, vsrc), reads=[bk], writes=["V"])

                        pending_norm = []
                        for qt in range(NQT):
                            t0 = qt * QT
                            xv = xfm.ap()[:, 0:8 * QT]
                            src3 = xfmd[s].rearrange("(c p) n -> p c n", p=128)[:, :, t0:t0 + QT]
                            em.op("pool", DMA(xv.rearrange("p (c n) -> p c n", n=QT), src3),
                                  writes=["xfm0", "xfm1"], dma="xfm0")
                            for c in range(4):
                                for nh in range(QT // 512):
                                    bk2 = "psSb.2"
                                    reg = psC.ap()[:, (nh % 2) * 512: (nh % 2) * 512 + 512]
                                    for dinc in range(8):
                                        em.op("pe",
                                              MM(reg,
                                                 wqh.ap()[:, dinc * 512 + c * 128: dinc * 512 + (c + 1) * 128],
                                                 xv[:, dinc * QT + nh * 512:dinc * QT + (nh + 1) * 512],
                                                 dinc == 0, dinc == 7),
                                              reads=["wqh", "xfm0", "xfm1"], writes=[bk2])
                                    em.op("dve",
                                          VCOPY(qp.ap()[0:64, (2 * c) * QT + nh * 512:(2 * c) * QT + nh * 512 + 512],
                                                reg[0:64, :]),
                                          reads=[bk2], writes=[f"qp.{2 * c}"])
                                    em.op("dve",
                                          VCOPY(qp.ap()[64:128, (2 * c + 1) * QT + nh * 512:(2 * c + 1) * QT + nh * 512 + 512],
                                                reg[64:128, :]),
                                          reads=[bk2], writes=[f"qp.{2 * c + 1}"])
                            for hh in range(8):
                                hg = half * 8 + hh
                                for kc in range(KC + 2):
                                    if kc < KC:
                                        sl = kc % 3
                                        for qh in range(QT // 512):
                                            em.op("pe",
                                                  MM([psA, psB, psC][sl].ap()[:, qh * 512:(qh + 1) * 512],
                                                     Kh.ap()[:, (hh // 2) * S + kc * 128:(hh // 2) * S + (kc + 1) * 128],
                                                     qp.ap()[:, hh * QT + qh * 512:hh * QT + (qh + 1) * 512],
                                                     True, True),
                                                  reads=["K", f"qp.{hh}"], writes=[f"psSb.{sl}"])
                                        em.op("act",
                                              EXPA(pbuf.ap()[:, sl * QT:(sl + 1) * QT],
                                                   [psA, psB, psC][sl].ap()[:, 0:QT],
                                                   cbi[:, 0:1], csc[:, 0:1]),
                                              reads=[f"psSb.{sl}", "csc", "cbi"],
                                              writes=[f"P{sl}"])
                                    if kc >= 2:
                                        kp = kc - 2
                                        slp = kp % 3
                                        for qh in range(QT // 512):
                                            em.op("pe",
                                                  MM(psD.ap()[0:65, qh * 512:(qh + 1) * 512],
                                                     Va.ap()[:, kp * 528 + hh * 66: kp * 528 + hh * 66 + 65],
                                                     pbuf.ap()[:, slp * QT + qh * 512: slp * QT + (qh + 1) * 512],
                                                     kp == 0, kp == KC - 1),
                                                  reads=[f"P{slp}", "V"], writes=[f"psO.{qh}"])
                                    if kc == min(7, KC - 1) and pending_norm:
                                        pending_norm.pop()()
                                em.op("dve", VCOPY(obuf[:], psD[0:65, 0:QT]),
                                      reads=["psO.0", "psO.1"], writes=["obuf"])
                                def _recip(e):
                                    with nc.allow_low_precision(reason="1/l feeds a bf16 matmul operand"):
                                        return e.reciprocal(rlb[0:1, 0:QT], obuf[64:65, 0:QT])
                                em.op("dve", _recip, reads=["obuf"], writes=["rlb"])

                                def mk_norm(hg=hg, t0=t0):
                                    def go():
                                        for qh in range(QT // 512):
                                            em.op("pe",
                                                  MM(psC.ap()[0:128, qh * 512:(qh + 1) * 512],
                                                     ones64[0:1, 0:128],
                                                     rlb[0:1, qh * 512:(qh + 1) * 512], True, True),
                                                  reads=["rlb", "ones64"],
                                                  writes=["psSb.2"])
                                        em.op("dve", VCOPY(bcs[:], psC[0:64, 0:QT]),
                                              reads=["psSb.2"], writes=["bcs"])
                                        arow = (hg % 2) * 64
                                        acol = (hg // 2) * S + t0
                                        em.op("dve",
                                              TT(afm.ap()[arow:arow + 64, acol:acol + QT],
                                                 obuf[0:64, 0:QT], bcs[:], OP.mult),
                                              reads=["obuf", "bcs"], writes=["attnfm"])
                                    return go
                                pending_norm.append(mk_norm())
                        while pending_norm:
                            pending_norm.pop()()
                    if DEBUG and s == 0:
                        em.op("sp", DMA(dbg["K"], Kh[:]), reads=["K"], dma="dbgK")
                        em.op("sp", DMA(dbg["V"], Va[:]), reads=["V"], dma="dbgV")
                        em.op("sp", DMA(dbg["qp"], qp[:]), reads=[f"qp.{i}" for i in range(8)], dma="dbgqp")
                    em.barrier()
                if DEBUG and s == 0:
                    em.op("sp", DMA(dbg["afm"], afm[:]), reads=["attnfm"], dma="dbgafm")

                with ExitStack() as ph:
                    EF = ph.enter_context
                    x1 = EF(nc.sbuf_tensor([128, TSUBS * D], F32))
                    x1tmb = EF(nc.sbuf_tensor([128, D], BF16))
                    x1fm = EF(nc.sbuf_tensor([128, 8 * FT], BF16))
                    yb = EF(nc.sbuf_tensor([128, TSUBS * D], F32))
                    hbuf = EF(nc.sbuf_tensor([128, 8 * FT], BF16))
                    ws = [EF(nc.sbuf_tensor(f"wstream_s{s}_{i}", [128, 8 * 512], BF16)) for i in range(3)]
                    xres = EF(nc.sbuf_tensor([128, 2 * D], F32))
                    psU, psS1, psS2, psV = psA, psB, psC, psD

                    ca_ones = combaug.ap().rearrange("p (t n) -> p t n", n=9)[:, :, 8:9]
                    em.op("dve", (lambda a=ca_ones: (lambda e: e.memset(a, 1.0)))(), writes=["combaug"])

                    wsctr = [0]

                    def stream(src_ap, key):
                        i = wsctr[0] % 3
                        wsctr[0] += 1
                        dst = ws[i].ap().rearrange("p (c n) -> p c n", c=8)
                        em.op("pool", DMA(dst, src_ap), writes=[f"ws{i}"], dma=f"ws{i}")
                        return ws[i], f"ws{i}"

                    def ln_batch():
                        em.op("dve", TS(gmean[:], zsum[:], 1.0 / D, None, OP.mult),
                              reads=["zsum"], writes=["gmean"])
                        em.op("dve", TT(gtmp[:], gmean[:], gmean[:], OP.mult),
                              reads=["gmean"], writes=["gtmp"])
                        em.op("dve", STT(gsc1[:], zsq[:], 1.0 / D, gtmp[:], OP.mult, OP.subtract),
                              reads=["zsq", "gtmp"], writes=["gsc1"])
                        em.op("act", (lambda e: e.activation(gtmp[:], gsc1[:], AF.Ln, bias=ceps[:, 0:1])),
                              reads=["gsc1", "ceps"], writes=["gtmp"])
                        em.op("dve", TS(gtmp[:], gtmp[:], -0.5, None, OP.mult),
                              reads=["gtmp"], writes=["gtmp"])
                        em.op("act", (lambda e: e.activation(grstd[:], gtmp[:], AF.Exp)),
                              reads=["gtmp"], writes=["grstd"])

                    for f in range(FCH):
                        wop = [stream(wo[:, :, dh * 512:(dh + 1) * 512], f"s{s}wo{f}.{dh}")
                               for dh in range(2)]
                        for ts in range(TSUBS):
                            tok0 = f * FT + ts * 128
                            xr = xres.ap()[:, (ts % 2) * D:((ts % 2) + 1) * D]
                            em.op("sp", DMA(xr, xtm[s, tok0:tok0 + 128, :]),
                                  writes=[f"xres{ts % 2}"], dma=f"xres{ts % 2}")
                            pw, pwk = (psU, "psU") if ts % 2 == 0 else (psS1, "psS1")
                            for dh in range(2):
                                wbuf, wkey = wop[dh]
                                for dinc in range(8):
                                    em.op("pe",
                                          MM(pw.ap()[:, dh * 512:(dh + 1) * 512],
                                             afm.ap()[:, dinc * S + tok0: dinc * S + tok0 + 128],
                                             wbuf.ap()[:, dinc * 512:(dinc + 1) * 512],
                                             dinc == 0, dinc == 7),
                                          reads=["attnfm", wkey], writes=[f"{pwk}.{dh}"])
                            x1s = x1.ap()[:, ts * D:(ts + 1) * D]
                            em.op("dve",
                                  STT(x1s, pw.ap()[:, 0:1024], 1.0, xr, OP.mult, OP.add,
                                      zsum[:, ts:ts + 1]),
                                  reads=[f"{pwk}.0", f"{pwk}.1", f"xres{ts % 2}"],
                                  writes=[f"x1.{ts}", "zsum"])
                            em.op("dve",
                                  STT(xr, x1s, 1.0, x1s, OP.mult, OP.mult, zsq[:, ts:ts + 1]),
                                  reads=[f"x1.{ts}"], writes=[f"xres{ts % 2}", "zsq"])
                        ln_batch()
                        for ts in range(TSUBS):
                            x1s = x1.ap()[:, ts * D:(ts + 1) * D]
                            em.op("dve",
                                  TS(x1s, x1s, gmean[:, ts:ts + 1], grstd[:, ts:ts + 1],
                                     OP.subtract, OP.mult),
                                  reads=["gmean", "grstd", f"x1.{ts}"], writes=[f"x1.{ts}"])
                            em.op("dve", VCOPY(x1tmb[:], x1s), reads=[f"x1.{ts}"], writes=["x1tmb"])
                            for dinc in range(8):
                                em.op("pe",
                                      MM(psV.ap()[:, dinc * 128:(dinc + 1) * 128],
                                         x1tmb.ap()[:, dinc * 128:(dinc + 1) * 128],
                                         identb[:], True, True),
                                      reads=["x1tmb", "ident"], writes=[f"psV.{dinc // 4}"])
                            fmdst = x1fm.ap().rearrange("p (c t) -> p c t", c=8)[:, :, ts * 128:(ts + 1) * 128]
                            fmsrc = psV.ap().rearrange("p (c t) -> p c t", c=8)
                            em.op("dve", VCOPY(fmdst, fmsrc),
                                  reads=["psV.0", "psV.1"], writes=["x1fm"])
                        if DEBUG and s == 0 and f == 0:
                            em.op("sp", DMA(dbg["x1"], x1[:]), reads=[f"x1.{t2}" for t2 in range(TSUBS)], dma="dbgx1")
                        for ts in range(TSUBS):
                            for dinc in range(8):
                                em.op("pe",
                                      MM(psS2.ap()[:, ts * 8:(ts + 1) * 8],
                                         x1fm.ap()[:, dinc * FT + ts * 128: dinc * FT + (ts + 1) * 128],
                                         gfmt.ap()[:, dinc * 8:(dinc + 1) * 8],
                                         dinc == 0, dinc == 7),
                                      reads=["x1fm", "gfmt"], writes=["psS2.0"])
                        em.op("act",
                              (lambda e: e.activation(scores[:, 0:8 * TSUBS], psS2[:, 0:8 * TSUBS], AF.Exp)),
                              reads=["psS2.0"], writes=["scores"])
                        for ts in range(TSUBS):
                            sl = scores.ap()[:, ts * 8:(ts + 1) * 8]
                            m1 = gm1[:, ts:ts + 1]
                            m2 = gm2[:, ts:ts + 1]
                            c1 = gsc1[:, ts:ts + 1]
                            t1 = gtmp[:, ts:ts + 1]
                            em.op("dve", RED(m1, sl, "max"), reads=["scores"], writes=["gm1"])
                            em.op("dve", TS(geq[:], sl, m1, None, OP.is_ge),
                                  reads=["scores", "gm1"], writes=["geq"])
                            em.op("dve", RED(c1, geq[:], "sum"), reads=["geq"], writes=["gsc1"])
                            em.op("dve", STT(gmask[:], geq[:], -1e30, sl, OP.mult, OP.add),
                                  reads=["geq", "scores"], writes=["gmask"])
                            em.op("dve", RED(m2, gmask[:], "max"), reads=["gmask"], writes=["gm2"])
                            em.op("dve", TS(t1, c1, 2.0, None, OP.is_ge),
                                  reads=["gsc1"], writes=["gtmp"])
                            em.op("dve", TT(c1, m1, m2, OP.subtract),
                                  reads=["gm1", "gm2"], writes=["gsc1"])
                            em.op("dve", STT(m2, t1, c1, m2, OP.mult, OP.add),
                                  reads=["gtmp", "gsc1", "gm2"], writes=["gm2"])
                            em.op("dve", TS(geq[:], sl, m2, None, OP.is_ge),
                                  reads=["scores", "gm2"], writes=["geq"])
                            em.op("dve", TT(gmask[:], sl, geq[:], OP.mult),
                                  reads=["scores", "geq"], writes=["gmask"])
                            em.op("dve", TT(t1, m1, m2, OP.add),
                                  reads=["gm1", "gm2"], writes=["gtmp"])
                            em.op("dve", (lambda o, i: (lambda e: e.reciprocal(o, i)))(c1, t1),
                                  reads=["gtmp"], writes=["gsc1"])
                            em.op("dve", TS(comb.ap()[:, ts * 8:(ts + 1) * 8], gmask[:], c1, None, OP.mult),
                                  reads=["gmask", "gsc1"], writes=["comb"])
                            em.op("dve", VCOPY(combaug[:, ts * 9:ts * 9 + 8], comb.ap()[:, ts * 8:(ts + 1) * 8]),
                                  reads=["comb"], writes=["combaug"])
                        if DEBUG and s == 0 and f == 0:
                            em.op("sp", DMA(dbg["scores"], scores[:]), reads=["scores"], dma="dbgsc")
                            em.op("sp", DMA(dbg["comb"], comb[:]), reads=["comb"], dma="dbgcb")
                        for e9 in [8] + list(range(8)):
                            for hh2 in range(2):
                                ubuf, ukey = stream(upw[e9][:, :, hh2 * 512:(hh2 + 1) * 512],
                                                    f"u{e9}.{hh2}")
                                for hcl in range(4):
                                    hc = hh2 * 4 + hcl
                                    pw, pwk = (psU, "psU") if hc % 2 == 0 else (psS1, "psS1")
                                    for th in range(FT // 512):
                                        for dinc in range(8):
                                            em.op("pe",
                                                  MM(pw.ap()[:, th * 512:(th + 1) * 512],
                                                     ubuf.ap()[:, dinc * 512 + hcl * 128: dinc * 512 + (hcl + 1) * 128],
                                                     x1fm.ap()[:, dinc * FT + th * 512: dinc * FT + (th + 1) * 512],
                                                     dinc == 0, dinc == 7),
                                                  reads=[ukey, "x1fm"], writes=[f"{pwk}.{th}"])
                                    em.op("act",
                                          (lambda d, srcap, bap: (lambda e: e.activation(
                                              d, srcap, AF.Silu, bias=bap)))(
                                              hbuf.ap()[:, hc * FT:(hc + 1) * FT],
                                              pw.ap()[:, 0:FT],
                                              ubfm.ap()[:, e9 * 8 + hc: e9 * 8 + hc + 1]),
                                          reads=[f"{pwk}.0"] + ([f"{pwk}.1"] if FT > 512 else []) + ["ubfm"],
                                          writes=[f"h.{hc}"])
                            if e9 == 8:
                                for ts in range(TSUBS):
                                    em.op("pe", MM(psV.ap()[0:NEXP, ts * 128:(ts + 1) * 128],
                                                   combaug.ap()[:, ts * 9:(ts + 1) * 9], identb[:], True, True),
                                          reads=["combaug", "ident"],
                                          writes=[f"psV.{(ts * 128) // 512}"])
                                em.op("dve", VCOPY(combT[:], psV[0:NEXP, 0:128 * TSUBS]),
                                      reads=["psV.0", "psV.1"], writes=["combT"])
                            dnp = [stream(dnw[e9][:, :, dh * 512:(dh + 1) * 512], f"d{e9}.{dh}")
                                   for dh in range(2)]
                            for ts in range(TSUBS):
                                pd, pdk = (psS2, "psS2") if ts % 2 == 0 else (psV, "psV")
                                for dh in range(2):
                                    dbuf, dkey = dnp[dh]
                                    for hc in range(8):
                                        em.op("pe",
                                              MM(pd.ap()[:, dh * 512:(dh + 1) * 512],
                                                 hbuf.ap()[:, hc * FT + ts * 128: hc * FT + (ts + 1) * 128],
                                                 dbuf.ap()[:, hc * 512:(hc + 1) * 512],
                                                 hc == 0, (hc == 7 and e9 != 8)),
                                              reads=[dkey, f"h.{hc}"], writes=[f"{pdk}.{dh}"])
                                    if e9 == 8:
                                        em.op("pe",
                                              MM(pd.ap()[:, dh * 512:(dh + 1) * 512],
                                                 combT.ap()[0:NEXP, ts * 128:(ts + 1) * 128],
                                                 db9.ap()[:, dh * 512:(dh + 1) * 512],
                                                 False, True),
                                              reads=["combT", "db9"], writes=[f"{pdk}.{dh}"])
                                ysl = yb.ap()[:, ts * D:(ts + 1) * D]
                                if e9 == 8:
                                    em.op("dve", VCOPY(ysl, pd[:, 0:1024]),
                                          reads=[f"{pdk}.0", f"{pdk}.1"], writes=[f"y.{ts}"])
                                else:
                                    cs = comb.ap()[:, ts * 8 + e9: ts * 8 + e9 + 1]
                                    em.op("dve",
                                          STT(ysl, pd.ap()[:, 0:1024], cs, ysl, OP.mult, OP.add),
                                          reads=[f"{pdk}.0", f"{pdk}.1", "comb", f"y.{ts}"], writes=[f"y.{ts}"])
                        if DEBUG and s == 0 and f == 0:
                            em.op("sp", DMA(dbg["y"], yb[:]), reads=[f"y.{t2}" for t2 in range(TSUBS)], dma="dbgy")
                        for ts in range(TSUBS):
                            x1s = x1.ap()[:, ts * D:(ts + 1) * D]
                            ysl = yb.ap()[:, ts * D:(ts + 1) * D]
                            em.op("dve",
                                  STT(x1s, ysl, 1.0, x1s, OP.mult, OP.add, zsum[:, ts:ts + 1]),
                                  reads=[f"y.{ts}", f"x1.{ts}"], writes=[f"x1.{ts}", "zsum"])
                            em.op("dve",
                                  STT(xres.ap()[:, (ts % 2) * D:((ts % 2) + 1) * D], x1s, 1.0, x1s,
                                      OP.mult, OP.mult, zsq[:, ts:ts + 1]),
                                  reads=[f"x1.{ts}"], writes=[f"xres{ts % 2}", "zsq"])
                        ln_batch()
                        for ts in range(TSUBS):
                            tok0 = f * FT + ts * 128
                            x1s = x1.ap()[:, ts * D:(ts + 1) * D]
                            em.op("dve",
                                  TS(x1s, x1s, gmean[:, ts:ts + 1], grstd[:, ts:ts + 1],
                                     OP.subtract, OP.mult),
                                  reads=["gmean", "grstd", f"x1.{ts}"], writes=[f"x1.{ts}"])
                            em.op("sp", DMA(outd[s, tok0:tok0 + 128, :], x1s),
                                  reads=[f"x1.{ts}"], dma="out")
                    em.barrier()

        with nc.Block() as block:
            em.replay(block)
    return nc


def _tile_w(w):
    return np.ascontiguousarray(w.reshape(8, 128, D).transpose(1, 0, 2))


def kernel(**inp):
    x = np.ascontiguousarray(np.asarray(inp["L0_attn_x"], dtype=np.float32))
    B, S, _ = x.shape
    wqh = _tile_w(np.asarray(inp["L0_attn_w_q"], dtype=np.float32))
    wkh = _tile_w(np.asarray(inp["L0_attn_w_k"], dtype=np.float32))
    wvh = _tile_w(np.asarray(inp["L0_attn_w_v"], dtype=np.float32))
    woh = _tile_w(np.asarray(inp["L0_attn_w_o"], dtype=np.float32))
    upw9 = np.concatenate([np.asarray(inp["L0_ffn_up_w"], dtype=np.float32),
                           np.asarray(inp["L0_ffn_sh_up_w"], dtype=np.float32)[None]], axis=0)
    dnw9 = np.concatenate([np.asarray(inp["L0_ffn_down_w"], dtype=np.float32),
                           np.asarray(inp["L0_ffn_sh_down_w"], dtype=np.float32)[None]], axis=0)
    upw_t = np.ascontiguousarray(upw9.reshape(NEXP, 8, 128, D).transpose(0, 2, 1, 3))
    dnw_t = np.ascontiguousarray(dnw9.reshape(NEXP, 8, 128, D).transpose(0, 2, 1, 3))
    upb9 = np.concatenate([np.asarray(inp["L0_ffn_up_b"], dtype=np.float32),
                           np.asarray(inp["L0_ffn_sh_up_b"], dtype=np.float32)[None]], axis=0)
    upbfm = np.ascontiguousarray(upb9.reshape(NEXP, 8, 128).transpose(2, 0, 1).reshape(128, 72))
    db9 = np.ascontiguousarray(np.concatenate(
        [np.asarray(inp["L0_ffn_down_b"], dtype=np.float32),
         np.asarray(inp["L0_ffn_sh_down_b"], dtype=np.float32)[None]], axis=0))
    gate = np.asarray(inp["L0_ffn_gate_w"], dtype=np.float32)
    gatefmt = np.ascontiguousarray(gate.reshape(8, 8, 128).transpose(2, 1, 0).reshape(128, 64))
    ident = np.eye(128, dtype=ml_dtypes.bfloat16)

    BC = B // 8
    nc = build_program(S=S, BC=BC, FT=1024)
    in_maps = []
    for c in range(8):
        xc = np.ascontiguousarray(x[c * BC:(c + 1) * BC])
        xfmc = np.ascontiguousarray(xc.transpose(0, 2, 1))
        in_maps.append(dict(x_tm=xc, x_fm=xfmc, wq=wqh, wk=wkh, wv=wvh, wo=woh,
                            upw=upw_t, dnw=dnw_t, gatefmt=gatefmt, upbfm=upbfm,
                            downb9=db9, ident=ident))
    res = run_bass_kernel_spmd(nc, in_maps, core_ids=list(range(8)))
    out = np.concatenate([res.results[i]["out"] for i in range(8)], axis=0)
    return out
